# version 14
# baseline (speedup 1.0000x reference)
import numpy as np
import ml_dtypes
from contextlib import ExitStack

import concourse.bass as bass
import concourse.mybir as mybir
from concourse.bass_utils import run_bass_kernel_spmd

F32 = mybir.dt.float32
BF16 = mybir.dt.bfloat16
AF = mybir.ActivationFunctionType
ALU = mybir.AluOpType
AX = mybir.AxisListType

NCORE = 8
S, D, HID, HPAD = 8192, 2048, 5461, 5504
NW = HID * D
NG, GRP = 16, 512
NH = 43
NR = 16
RSQ = float(np.float32(1.0) / np.sqrt(np.float32(128.0)))
EPS = 1e-5
RGRP = [list(range(NCORE))]


def _hw(hc):
    return 85 if hc == NH - 1 else 128


def build_program():
    nc = bass.Bass("TRN2", target_bir_lowering=False, debug=False,
                   detect_race_conditions=False)

    xtm = nc.dram_tensor("xtm", [S, D], F32, kind="ExternalInput").ap()
    xfm = nc.dram_tensor("xfm", [D, S], F32, kind="ExternalInput").ap()
    xfo = nc.dram_tensor("xfo", [D, 1024], F32, kind="ExternalInput").ap()
    wqkv = nc.dram_tensor("wqkv", [D, 768], F32, kind="ExternalInput").ap()
    wout = nc.dram_tensor("wout", [D, D], F32, kind="ExternalInput").ap()
    w1sl = nc.dram_tensor("w1sl", [256, HID], F32, kind="ExternalInput").ap()
    w3sl = nc.dram_tensor("w3sl", [256, HID], F32, kind="ExternalInput").ap()
    w2sl = nc.dram_tensor("w2sl", [688, D], F32, kind="ExternalInput").ap()
    idn = nc.dram_tensor("idn", [128, 128], F32, kind="ExternalInput").ap()
    idnn = nc.dram_tensor("idnn", [128, 128], F32, kind="ExternalInput").ap()
    maskm = nc.dram_tensor("maskm", [4, 128, 512], BF16,
                           kind="ExternalInput").ap()
    y = nc.dram_tensor("y", [D, 1024], F32, kind="ExternalOutput").ap()

    g_w1i = nc.dram_tensor("g_w1i", [256, HID], BF16).ap()
    g_w1o = nc.dram_tensor("g_w1o", [D, HID], BF16).ap()
    g_w3i = nc.dram_tensor("g_w3i", [256, HID], BF16).ap()
    g_w3o = nc.dram_tensor("g_w3o", [D, HID], BF16).ap()
    g_w2i = nc.dram_tensor("g_w2i", [688, D], BF16).ap()
    g_w2o = nc.dram_tensor("g_w2o", [HPAD, D], BF16).ap()
    g_si = nc.dram_tensor("g_si", [1, 64], F32).ap()
    g_so = nc.dram_tensor("g_so", [1, 64], F32).ap()
    ao_i = nc.dram_tensor("ao_i", [D, 1024], BF16).ap()
    ao_o = nc.dram_tensor("ao_o", [NCORE * D, 1024], BF16).ap()
    xnt = nc.dram_tensor("xnt", [D, S], BF16).ap()
    woutb = nc.dram_tensor("woutb", [D, D], BF16).ap()

    sched = {e: [] for e in ("sy", "gp", "pe", "act", "dve")}
    cnt = {"sy": 0, "gp": 0, "pe": 0, "act": 0, "dve": 0, "pool": 0,
           "coll": 0}

    def emit(eng, fn, waits=(), incs=()):
        stream = "gp" if eng == "pool" else eng
        sched[stream].append((tuple(waits), fn, tuple(incs)))

    NSEM = {"sy": 8, "gp": 4}

    def dma(q, fn, waits=()):
        cnt[q] += 1
        sem = "%s%d" % (q, cnt[q] % NSEM[q])
        cnt[sem] = cnt.get(sem, 0) + 16
        emit(q, fn, waits, ((sem, 16),))
        return (sem, cnt[sem])

    def step(eng, fn, waits=()):
        cnt[eng] += 1
        emit(eng, fn, waits, ((eng, 1),))
        return (eng, cnt[eng])

    def ccall(fn, waits=()):
        cnt["coll"] += 1
        emit("gp", fn, waits, (("coll", 1),))
        return ("coll", cnt["coll"])

    class Buf:
        __slots__ = ("prod", "cons")

        def __init__(s):
            s.prod = None
            s.cons = {}

        def w(s):
            ws = list(s.cons.items())
            if s.prod:
                ws.append(s.prod)
            s.cons = {}
            return ws

        def r(s):
            return [s.prod] if s.prod else []

        def rd(s, tok):
            sem, v = tok
            if s.cons.get(sem, -1) < v:
                s.cons[sem] = v

    ctx = ExitStack()
    sb = lambda name, shape, dt: ctx.enter_context(
        nc.sbuf_tensor(name, shape, dt))
    ps = lambda name, shape: ctx.enter_context(
        nc.psum_tensor(name, shape, F32))

    ST = [ps("ST0", [128, 1024]), ps("ST1", [128, 1024])]
    AO0 = ps("AO0", [128, 512])
    AO1 = ps("AO1", [128, 512])
    PAC = ps("PAC", [128, 512])
    PAC2 = ps("PAC2", [128, 512])
    bST = [Buf(), Buf()]
    bAO0, bAO1, bPAC, bPAC2 = Buf(), Buf(), Buf(), Buf()
    pbuf = {id(ST[0]): bST[0], id(ST[1]): bST[1], id(AO0): bAO0,
            id(AO1): bAO1, id(PAC): bPAC, id(PAC2): bPAC2}

    I32 = sb("I32", [128, 128], F32)
    I32N = sb("I32N", [128, 128], F32)
    LNN = sb("LNN", [128, 1], F32)
    ONES1 = sb("ONES1", [1, 128], F32)
    ONESC = sb("ONESC", [128, 1], F32)
    EPSC = sb("EPSC", [128, 1], F32)
    MASKS = sb("MASKS", [128, 4 * 512], BF16)
    WQKVb = sb("WQKVb", [128, NR * 6 * 128], BF16)
    SROW = sb("SROW", [1, 64], F32)
    PROW = sb("PROW", [1, 16], F32)
    PARC = sb("PARC", [128, 16], F32)
    S23C = sb("S23C", [128, 1], F32)
    ARROW = sb("ARROW", [1, 64], F32)
    bPARC, bS23, bPROW, bSROW, bAR = Buf(), Buf(), Buf(), Buf(), Buf()

    actx = ExitStack()
    sba = lambda name, shape, dt: actx.enter_context(
        nc.sbuf_tensor(name, shape, dt))
    XT = sba("XT", [128, D], F32)
    XF = sba("XF", [128, NR * GRP], F32)
    XNG = sba("XNG", [128, NR * GRP], BF16)
    TMP = sba("TMP", [128, 3 * GRP], F32)
    KF = [sba("KF0", [128, S], BF16), sba("KF1", [128, S], BF16)]
    VT = [sba("VT0", [128, S], BF16), sba("VT1", [128, S], BF16)]
    QF = sba("QF", [128, 2 * 1024], BF16)
    PB = sba("PB", [128, 3 * 1024], BF16)
    LD = sba("LD", [128, 1024], F32)
    LP = sba("LP", [128, 1024], F32)
    LINV = sba("LINV", [1, 1024], F32)
    LBS = sba("LBS", [128, 1024], F32)
    AOB = sba("AOB", [128, 2 * 1024], BF16)
    AOR = sba("AOR", [128, 1024], F32)
    SUMS = sba("SUMS", [128, 4], F32)
    SQC = sba("SQC", [128, 4], F32)
    T1S = sba("T1S", [128, 4], F32)
    VPE = sba("VPE", [128, 4], F32)
    SD4 = sba("SD4", [128, 4], F32)
    RST = sba("RST", [128, 4], F32)
    M4 = sba("M4", [128, 4], F32)
    MB = sba("MB", [128, 512], F32)
    RB = sba("RB", [128, 512], F32)
    WTl = sba("WTl", [128, 2 * 512], F32)
    NEGT = sba("NEGT", [128, 2 * 1024], BF16)
    TERNT = sba("TERNT", [128, 2 * 512], BF16)
    RCOLS = sba("RCOLS", [128, 72], F32)
    PC = sba("PC", [128, 4], F32)

    bXT = [Buf(), Buf()]
    bXF = [Buf(), Buf()]
    bXNG = Buf()
    bQF = [Buf(), Buf()]
    bPB = [Buf(), Buf(), Buf()]
    bL = Buf()
    bLINV = Buf()
    bLBS = Buf()
    bAOB = [Buf(), Buf()]
    bAOR = Buf()
    tmp_lastmul = [None]
    bKF = {}
    bVT = {}
    bMB = Buf()
    bRB = Buf()
    bSUMS, bSQC, bT1S, bVPE, bSD4, bRST, bM4 = (Buf(), Buf(), Buf(), Buf(),
                                                Buf(), Buf(), Buf())
    bWT = [Buf(), Buf()]
    bNEG = [Buf(), Buf()]
    bTERN = [Buf(), Buf()]
    bRC = Buf()
    bPCb = Buf()

    pid_holder = {}

    step("dve", lambda e: e.memset(ONES1[:], 1.0))
    step("dve", lambda e: e.memset(ONESC[:], 1.0))
    step("dve", lambda e: e.memset(EPSC[:], EPS * float(D) * float(D)))
    v_ar0 = step("dve", lambda e: e.memset(ARROW[:], 0.0))
    v_rc0 = step("dve", lambda e: e.memset(RCOLS[:], 0.0))
    step("dve", lambda e: e.memset(LNN[:], float(np.log(2048.0))))
    bRC.prod = v_rc0
    v_i32 = dma("sy", lambda e: e.dma_start(out=I32[:], in_=idn[:]))
    v_i32n = dma("sy", lambda e: e.dma_start(out=I32N[:], in_=idnn[:]))
    v_msk = dma("sy", lambda e: e.dma_start(
        out=MASKS[:].rearrange("p (m f) -> p m f", f=512),
        in_=maskm.rearrange("m p f -> p m f")))
    v_wq = dma("gp", lambda e: e.dma_start(
        out=WQKVb[:].rearrange("p (r j f) -> p r j f", j=6, f=128),
        in_=wqkv.rearrange("(r p) (j f) -> p r j f", p=128, f=128)))
    v_wob = dma("gp", lambda e: e.dma_start(out=woutb[:], in_=wout[:]))

    def tern_tiles(mat):
        if mat in ("w1", "w3"):
            rows, cols = 256, HID
        else:
            rows, cols = 688, D
        out = []
        for pr0 in range(0, rows, 128):
            prc = min(128, rows - pr0)
            for cc0 in range(0, cols, 512):
                cw = min(512, cols - cc0)
                out.append((mat, pr0, prc, cc0, cw))
        return out

    MATSRC = {"w1": w1sl, "w3": w3sl, "w2": w2sl}
    MATDST = {"w1": g_w1i, "w3": g_w3i, "w2": g_w2i}
    MATIDX = {"w1": 0, "w3": 1, "w2": 2}
    p1_tasks = tern_tiles("w1") + tern_tiles("w3") + tern_tiles("w2")
    p2_tasks = list(p1_tasks)
    p1_done = []
    p2_done = {"w1": None, "w3": None, "w2": None}

    def emit_p1(task, ti):
        mat, pr0, prc, cc0, cw = task
        sl = bXTslot = ti % 2
        src = MATSRC[mat]
        wb = bWT[sl]
        v = dma("gp", (lambda e, s=sl, m=mat, a=pr0, b=prc, c=cc0, d=cw:
                       e.dma_start(out=WTl[0:b, s * 512:s * 512 + d],
                                   in_=MATSRC[m][a:a + b, c:c + d])),
                waits=wb.w())
        wb.prod = v
        t = step("dve", (lambda e, s=sl, b=prc, d=cw, i=ti:
                         e.tensor_reduce(RCOLS[0:b, i:i + 1],
                                         WTl[0:b, s * 512:s * 512 + d],
                                         AX.X, ALU.add,
                                         apply_absolute_value=True)),
                 waits=wb.r() + bRC.r())
        wb.rd(t)
        bRC.rd(t)
        p1_done.append(t)

    def emit_ar_chain():
        c_w1 = len(tern_tiles("w1"))
        c_w3 = len(tern_tiles("w3"))
        rng = {"w1": (0, c_w1), "w3": (c_w1, c_w1 + c_w3),
               "w2": (c_w1 + c_w3, len(p1_tasks))}
        toks = []
        for m in ("w1", "w3", "w2"):
            a, b = rng[m]
            i = MATIDX[m]
            t = step("dve", (lambda e, a=a, b=b, i=i:
                             e.tensor_reduce(PC[:, i:i + 1],
                                             RCOLS[:, a:b], AX.X, ALU.add)),
                    waits=list(p1_done))
            toks.append(t)
        bPCb.prod = toks[-1]
        mmw = bPAC2.w()
        for m in ("w1", "w3", "w2"):
            i = MATIDX[m]
            t = step("pe", (lambda e, i=i:
                            e.matmul(PAC2[0:1, i:i + 1], PC[:, i:i + 1],
                                     ONESC[:], start=True, stop=True)),
                     waits=(mmw if i == 0 else []) + toks)
        bPAC2.prod = t
        t2 = step("dve", lambda e: e.tensor_copy(ARROW[0:1, 0:3],
                                                 PAC2[0:1, 0:3]),
                  waits=bPAC2.r() + [v_ar0])
        bPAC2.rd(t2)
        bAR.prod = t2
        v = dma("gp", lambda e: e.dma_start(out=g_si[:], in_=ARROW[:]),
                waits=bAR.r())
        c1 = ccall(lambda e: e.collective_compute(
            "AllReduce", ALU.add, replica_groups=RGRP,
            ins=[g_si[:].opt()], outs=[g_so[:].opt()]), waits=[v])
        v2 = dma("sy", lambda e: e.dma_start(out=SROW[:], in_=g_so[:]),
                 waits=[c1])
        bSROW.prod = v2
        t3 = step("dve", lambda e: e.tensor_scalar(
            PROW[0:1, 0:3], SROW[0:1, 0:3], 0.5 / NW, None, ALU.mult),
            waits=bSROW.r())
        t4 = step("dve", lambda e: e.tensor_scalar(
            PROW[0:1, 4:7], SROW[0:1, 0:3], 0.5 / NW, None, ALU.mult),
            waits=bSROW.r())
        t5 = step("act", lambda e: e.mul(PROW[0:1, 8:11], SROW[0:1, 0:3],
                                         -0.5 / NW), waits=bSROW.r())
        bPROW.prod = t5
        mmw = bPAC.w()
        t6 = step("pe", lambda e: e.matmul(PAC[:, 0:16], ONES1[:],
                                           PROW[0:1, 0:16], start=True,
                                           stop=True),
                  waits=mmw + [t3, t4, t5, v_i32])
        bPAC.prod = t6
        t7 = step("dve", lambda e: e.tensor_copy(PARC[:], PAC[:, 0:16]),
                  waits=bPAC.r())
        bPAC.rd(t7)
        bPARC.prod = t7
        t8 = step("pool", lambda e: e.tensor_tensor(
            S23C[:], PARC[:, 5:6], PARC[:, 6:7], ALU.mult),
            waits=bPARC.r())
        bS23.prod = t8

    def emit_p2(task, ti):
        mat, pr0, prc, cc0, cw = task
        sl = ti % 2
        i = MATIDX[mat]
        wb, nb, tb = bWT[sl], bNEG[sl], bTERN[sl]
        v = dma("gp", (lambda e, s=sl, m=mat, a=pr0, b=prc, c=cc0, d=cw:
                       e.dma_start(out=WTl[0:b, s * 512:s * 512 + d],
                                   in_=MATSRC[m][a:a + b, c:c + d])),
                waits=wb.w())
        wb.prod = v
        nw_ = nb.w()
        ta = step("act", (lambda e, s=sl, b=prc, d=cw, i=i:
                          e.activation(NEGT[0:b, s * 1024:s * 1024 + d],
                                       WTl[0:b, s * 512:s * 512 + d],
                                       AF.Sign, bias=PARC[0:b, 8 + i:9 + i])),
                  waits=wb.r() + nw_ + bPARC.r())
        tb_ = step("act", (lambda e, s=sl, b=prc, d=cw, i=i:
                           e.activation(NEGT[0:b, s * 1024 + 512:
                                             s * 1024 + 512 + d],
                                        WTl[0:b, s * 512:s * 512 + d],
                                        AF.Sign, bias=PARC[0:b, i:i + 1])),
                   waits=[])
        wb.rd(tb_)
        nb.prod = tb_
        bPARC.rd(tb_)
        t2 = step("dve", (lambda e, s=sl, b=prc, d=cw:
                          e.tensor_tensor(
                              TERNT[0:b, s * 512:s * 512 + d],
                              NEGT[0:b, s * 1024:s * 1024 + d],
                              NEGT[0:b, s * 1024 + 512:s * 1024 + 512 + d],
                              ALU.add)),
                  waits=nb.r() + tb.w())
        nb.rd(t2)
        tb.prod = t2
        v2 = dma("gp", (lambda e, s=sl, m=mat, a=pr0, b=prc, c=cc0, d=cw:
                        e.dma_start(out=MATDST[m][a:a + b, c:c + d],
                                    in_=TERNT[0:b, s * 512:s * 512 + d])),
                 waits=tb.r())
        tb.rd(v2)
        p2_done[mat] = v2

    ag_done = {}

    def emit_ag(mat):
        src = {"w1": (g_w1i, g_w1o), "w3": (g_w3i, g_w3o),
               "w2": (g_w2i, g_w2o)}[mat]
        k = cnt["coll"]
        c = ccall((lambda e, a=src[0], b=src[1]: e.collective_compute(
            "AllGather", ALU.bypass, replica_groups=RGRP,
            ins=[a[:].opt()], outs=[b[:].opt()])),
            waits=[p2_done[mat], ("coll", k)])
        ag_done[mat] = c

    norm_done = {}
    norm_muls = {}
    xnt_stored = {}

    def emit_stats_part(g):
        red_toks, sq_toks = [], []
        for t in range(4):
            xb = bXT[0]
            v = dma("sy", (lambda e, g=g, t=t: e.dma_start(
                out=XT[:],
                in_=xtm[g * GRP + t * 128: g * GRP + (t + 1) * 128, :])),
                waits=xb.w())
            xb.prod = v
            r1 = step("dve", (lambda e, t=t: e.tensor_reduce(
                SUMS[:, t:t + 1], XT[:], AX.X, ALU.add)),
                waits=xb.r() + (bSUMS.w() if t == 0 else []))
            r2 = step("act", (lambda e, t=t: e.activation(
                TMP[:, 0:1024].bitcast(BF16), XT[:], AF.Square,
                accum_out=SQC[:, t:t + 1])),
                waits=xb.r() + (bSQC.w() if t == 0 else []) +
                ([tmp_lastmul[0]] if tmp_lastmul[0] else []))
            xb.rd(r1)
            xb.rd(r2)
            red_toks.append(r1)
            sq_toks.append(r2)
        bSUMS.prod = red_toks[-1]
        bSQC.prod = sq_toks[-1]
        t1 = step("pool", lambda e: e.tensor_tensor(
            T1S[:], SUMS[:], SUMS[:], ALU.mult),
            waits=bSUMS.r() + bT1S.w())
        bT1S.prod = t1
        t2 = step("dve", lambda e: e.scalar_tensor_tensor(
            VPE[:], SQC[:], float(D), T1S[:], ALU.mult, ALU.subtract),
            waits=bSQC.r() + bT1S.r() + bVPE.w())
        bT1S.rd(t2)
        bVPE.prod = t2
        t3 = step("act", lambda e: e.activation(SD4[:], VPE[:], AF.Ln,
                                                bias=EPSC[:]),
                  waits=bVPE.r() + bSD4.w())
        bVPE.rd(t3)
        bSD4.prod = t3
        t3b = step("dve", lambda e: e.tensor_scalar(
            T1S[:], SD4[:], -0.5, None, ALU.mult),
            waits=bSD4.r() + bT1S.w())
        bSD4.rd(t3b)
        bT1S.prod = t3b
        t4 = step("act", lambda e: e.activation(RST[:], T1S[:], AF.Exp,
                                                bias=LNN[:]),
                  waits=bT1S.r() + bRST.w())
        bT1S.rd(t4)
        bRST.prod = t4

    def emit_norm_part(g):
        mmw = bPAC.w()
        for t in range(4):
            tok = step("pe", (lambda e, t=t: e.matmul(
                PAC[:, t * 128:(t + 1) * 128],
                SUMS[:, t:t + 1].broadcast_to([128, 128]), I32N[:],
                start=True, stop=True)),
                waits=(mmw if t == 0 else []) + bSUMS.r() + [v_i32n])
        bSUMS.rd(tok)
        bPAC.prod = tok
        mmw = bPAC2.w()
        for t in range(4):
            tok = step("pe", (lambda e, t=t: e.matmul(
                PAC2[:, t * 128:(t + 1) * 128],
                RST[:, t:t + 1].broadcast_to([128, 128]), I32[:],
                start=True, stop=True)),
                waits=(mmw if t == 0 else []) + bRST.r())
        bRST.rd(tok)
        bPAC2.prod = tok
        e1 = step("act", lambda e: e.activation(MB[:], PAC[:], AF.Copy),
                  waits=bPAC.r() + bMB.w())
        bPAC.rd(e1)
        bMB.prod = e1
        e2 = step("act", lambda e: e.activation(RB[:], PAC2[:], AF.Copy),
                  waits=bPAC2.r() + bRB.w())
        bPAC2.rd(e2)
        bRB.prod = e2
        vh = [None, None]
        for h in range(2):
            xb = bXF[h]
            vh[h] = dma("sy", (lambda e, g=g, h=h: e.dma_start(
                out=XF[:].rearrange("p (r f) -> p r f", f=GRP)
                    [:, h * 8:(h + 1) * 8, :],
                in_=xfm.rearrange("(r p) q -> p r q", p=128)
                    [:, h * 8:(h + 1) * 8, g * GRP:(g + 1) * GRP])),
                waits=xb.w())
            xb.prod = vh[h]
        xng_w = bXNG.w()
        subs, muls = {}, {}
        order = []
        pend = []
        for r in range(NR):
            order.append(("s", r))
            pend.append(r)
            if r >= 2:
                order.append(("m", pend.pop(0)))
        while pend:
            order.append(("m", pend.pop(0)))
        last_mul = None
        for kind, r in order:
            h = 0 if r < 8 else 1
            if kind == "s":
                tok = step("pool", (lambda e, r=r: e.tensor_tensor(
                    TMP[:, (r % 3) * GRP:(r % 3 + 1) * GRP],
                    XF[:, r * GRP:(r + 1) * GRP], MB[:], ALU.subtract)),
                    waits=bXF[h].r() + bMB.r() + (xng_w if r == 0 else []))
                bXF[h].rd(tok)
                bMB.rd(tok)
                subs[r] = tok
            else:
                tok = step("dve", (lambda e, r=r: e.tensor_tensor(
                    XNG[:, r * GRP:(r + 1) * GRP],
                    TMP[:, (r % 3) * GRP:(r % 3 + 1) * GRP], RB[:],
                    ALU.mult)),
                    waits=bRB.r() + [subs[r]])
                bRB.rd(tok)
                muls[r] = tok
                last_mul = tok
        bXNG.prod = last_mul
        norm_done[g] = last_mul
        norm_muls[g] = dict(muls)
        tmp_lastmul[0] = last_mul
        v = dma("sy", (lambda e, g=g: e.dma_start(
            out=xnt.rearrange("(r p) q -> p r q", p=128)
                [:, :, g * GRP:(g + 1) * GRP],
            in_=XNG[:].rearrange("p (r f) -> p r f", f=GRP))),
            waits=[last_mul])
        bXNG.rd(v)
        xnt_stored[g] = v

    qkv_last_mm = {}

    def emit_qkv(g):
        nm = norm_muls[g]
        for j in range(4):
            pt = PAC if j % 2 == 0 else PAC2
            pb = pbuf[id(pt)]
            mmw = pb.w()
            for r in range(NR):
                tok = step("pe", (lambda e, pt=pt, r=r, j=j: e.matmul(
                    pt[:], WQKVb[:, (r * 6 + j) * 128:(r * 6 + j + 1) * 128],
                    XNG[:, r * GRP:(r + 1) * GRP],
                    start=(r == 0), stop=(r == NR - 1))),
                    waits=(mmw if r == 0 else []) + [nm[r], v_wq])
                bXNG.rd(tok)
            pb.prod = tok
            if j < 2:
                ev = step("act", (lambda e, pt=pt, g=g, j=j: e.activation(
                    QF[:, (g % 2) * 1024 + j * 512:
                       (g % 2) * 1024 + (j + 1) * 512],
                    pt[:], AF.Copy, scale=RSQ)),
                    waits=pb.r() + (bQF[g % 2].w() if j == 0 else []))
                bQF[g % 2].prod = ev
            else:
                ev = step("act", (lambda e, pt=pt, g=g, j=j: e.activation(
                    KF[j - 2][:, g * GRP:(g + 1) * GRP], pt[:], AF.Copy)),
                    waits=pb.r())
            pb.rd(ev)
        bKF[g] = ev
        for t in range(4):
            pt = PAC if t % 2 == 0 else PAC2
            pb = pbuf[id(pt)]
            mmw = pb.w()
            for r in range(NR):
                tok = step("pe", (lambda e, pt=pt, r=r, t=t: e.matmul(
                    pt[:, 0:256],
                    XNG[:, r * GRP + t * 128:r * GRP + (t + 1) * 128],
                    WQKVb[:, (r * 6 + 4) * 128:(r * 6 + 6) * 128],
                    start=(r == 0), stop=(r == NR - 1))),
                    waits=(mmw if r == 0 else []) + [nm[r], v_wq])
                bXNG.rd(tok)
            pb.prod = tok
            kc = 4 * g + t
            ev1 = step("act", (lambda e, pt=pt, kc=kc: e.activation(
                VT[0][:, kc * 128:(kc + 1) * 128], pt[:, 0:128], AF.Copy)),
                waits=pb.r())
            ev2 = step("act", (lambda e, pt=pt, kc=kc: e.activation(
                VT[1][:, kc * 128:(kc + 1) * 128], pt[:, 128:256], AF.Copy)),
                waits=pb.r())
            pb.rd(ev1)
            pb.rd(ev2)
        bVT[g] = ev2
        qkv_last_mm[g] = tok

    ao_stored = []

    def emit_flash(g, bg=None):
        nkc = 4 * g + 4
        first_l = {"dve": True, "pool": True}
        lw = bL.w()

        def emit_st(kc):
            b = kc % 2
            stw = bST[b].w()
            kv_w = [bKF[kc // 4], bVT[kc // 4]]
            m1 = step("pe", (lambda e, b=b, kc=kc, g=g: e.matmul(
                ST[b][:, 0:512], KF[0][:, kc * 128:(kc + 1) * 128],
                QF[:, (g % 2) * 1024:(g % 2) * 1024 + 512],
                start=True, stop=True)),
                waits=stw + kv_w + bQF[g % 2].r())
            m2 = step("pe", (lambda e, b=b, kc=kc, g=g: e.matmul(
                ST[b][:, 512:1024], KF[1][:, kc * 128:(kc + 1) * 128],
                QF[:, (g % 2) * 1024 + 512:(g % 2) * 1024 + 1024],
                start=True, stop=True)),
                waits=[])
            bQF[g % 2].rd(m2)
            bST[b].prod = m2

        def emit_exp(kc):
            b = kc % 2
            pb = kc % 3
            diag = kc >= 4 * g
            a = step("act", (lambda e, b=b, pb=pb: e.activation(
                PB[:, pb * 1024:(pb + 1) * 1024], ST[b][:], AF.Exp)),
                waits=bST[b].r() + bPB[pb].w())
            bST[b].rd(a)
            bPB[pb].prod = a
            gate = [a]
            if diag:
                j = kc - 4 * g
                d1 = step("dve", (lambda e, pb=pb, j=j: e.tensor_tensor(
                    PB[:, pb * 1024:pb * 1024 + 512],
                    PB[:, pb * 1024:pb * 1024 + 512],
                    MASKS[:, j * 512:(j + 1) * 512], ALU.mult)),
                    waits=[a, v_msk])
                d2 = step("dve", (lambda e, pb=pb, j=j: e.tensor_tensor(
                    PB[:, pb * 1024 + 512:pb * 1024 + 1024],
                    PB[:, pb * 1024 + 512:pb * 1024 + 1024],
                    MASKS[:, j * 512:(j + 1) * 512], ALU.mult)),
                    waits=[])
                bPB[pb].prod = d2
                gate = [d2]
            return gate

        def emit_av_l(kc, gate):
            pb = kc % 3
            av_w = (bAO0.w() + bAO1.w()) if kc == 0 else []
            v1 = step("pe", (lambda e, pb=pb, kc=kc, nkc=nkc: e.matmul(
                AO0[:], VT[0][:, kc * 128:(kc + 1) * 128],
                PB[:, pb * 1024:pb * 1024 + 512],
                start=(kc == 0), stop=(kc == nkc - 1))),
                waits=gate + av_w)
            v2 = step("pe", (lambda e, pb=pb, kc=kc, nkc=nkc: e.matmul(
                AO1[:], VT[1][:, kc * 128:(kc + 1) * 128],
                PB[:, pb * 1024 + 512:pb * 1024 + 1024],
                start=(kc == 0), stop=(kc == nkc - 1))),
                waits=[])
            bPB[pb].rd(v2)
            leng = "pool" if kc % 3 == 2 else "dve"
            LT = LD if leng == "dve" else LP
            if first_l[leng]:
                lt = step(leng, (lambda e, LT=LT, pb=pb: e.tensor_copy(
                    LT[:], PB[:, pb * 1024:(pb + 1) * 1024])),
                    waits=gate + lw)
                first_l[leng] = False
            else:
                lt = step(leng, (lambda e, LT=LT, pb=pb: e.tensor_tensor(
                    LT[:], LT[:], PB[:, pb * 1024:(pb + 1) * 1024],
                    ALU.add)),
                    waits=gate)
            bPB[pb].rd(lt)
            return v1, v2, lt

        l_dve = l_pool = None
        emit_st(0)
        for kc in range(nkc):
            gate = emit_exp(kc)
            if kc + 1 < nkc:
                emit_st(kc + 1)
            v1, v2, lt = emit_av_l(kc, gate)
            if lt[0] == "dve":
                l_dve = lt
            else:
                l_pool = lt
            if bg is not None and kc in bg:
                bg[kc]()
        bAO0.prod = v1
        bAO1.prod = v2
        arw = bAOR.w()
        ar1 = step("act", lambda e: e.activation(AOR[:, 0:512], AO0[:],
                                                 AF.Copy),
                   waits=bAO0.r() + arw)
        bAO0.rd(ar1)
        ar2 = step("act", lambda e: e.activation(AOR[:, 512:1024], AO1[:],
                                                 AF.Copy),
                   waits=bAO1.r())
        bAO1.rd(ar2)
        bAOR.prod = ar2
        for h, pt in ((0, PAC2), (1, PAC)):
            pb2 = pbuf[id(pt)]
            mmw = pb2.w()
            t1 = step("pe", (lambda e, pt=pt, h=h: e.matmul(
                pt[0:1, 0:512], ONESC[:], LD[:, h * 512:(h + 1) * 512],
                start=True, stop=False)),
                waits=mmw + [l_dve, l_pool])
            t2 = step("pe", (lambda e, pt=pt, h=h: e.matmul(
                pt[0:1, 0:512], ONESC[:], LP[:, h * 512:(h + 1) * 512],
                start=False, stop=True)),
                waits=[])
            pb2.prod = t2
        bL.rd(t2)
        r1 = step("dve", lambda e: e.reciprocal(LINV[0:1, 0:512],
                                                PAC2[0:1, 0:512]),
                  waits=bPAC2.r() + bLINV.w())
        bPAC2.rd(r1)
        r2 = step("dve", lambda e: e.reciprocal(LINV[0:1, 512:1024],
                                                PAC[0:1, 0:512]),
                  waits=bPAC.r())
        bPAC.rd(r2)
        bLINV.prod = r2
        for h, pt in ((0, PAC2), (1, PAC)):
            pb2 = pbuf[id(pt)]
            mmw = pb2.w()
            t = step("pe", (lambda e, pt=pt, h=h: e.matmul(
                pt[:], ONES1[:], LINV[0:1, h * 512:(h + 1) * 512],
                start=True, stop=True)),
                waits=mmw + bLINV.r())
            bLINV.rd(t)
            pb2.prod = t
        lb_w = bLBS.w()
        e1 = step("dve", lambda e: e.tensor_copy(LBS[:, 0:512],
                                                 PAC2[:, 0:512]),
                  waits=bPAC2.r() + lb_w)
        bPAC2.rd(e1)
        e2 = step("dve", lambda e: e.tensor_copy(LBS[:, 512:1024],
                                                 PAC[:, 0:512]),
                  waits=bPAC.r())
        bPAC.rd(e2)
        bLBS.prod = e2
        ab = bAOB[g % 2]
        abw = ab.w()
        n1 = step("dve", (lambda e, g=g: e.tensor_tensor(
            AOB[:, (g % 2) * 1024:(g % 2) * 1024 + 512], AOR[:, 0:512],
            LBS[:, 0:512], ALU.mult)),
            waits=bAOR.r() + bLBS.r() + abw)
        bAOR.rd(n1)
        n2 = step("dve", (lambda e, g=g: e.tensor_tensor(
            AOB[:, (g % 2) * 1024 + 512:(g % 2) * 1024 + 1024],
            AOR[:, 512:1024],
            LBS[:, 512:1024], ALU.mult)),
            waits=bAOR.r() + bLBS.r())
        bAOR.rd(n2)
        bLBS.rd(n2)
        ab.prod = n2
        j = g // 2
        for hl, tok in ((0, n1), (1, n2)):
            v = dma("sy", (lambda e, g=g, j=j, hl=hl: e.dma_start(
                out=ao_i[j * 256 + hl * 128:j * 256 + (hl + 1) * 128,
                         (g % 2) * 512:(g % 2 + 1) * 512],
                in_=AOB[:, (g % 2) * 1024 + hl * 512:
                        (g % 2) * 1024 + (hl + 1) * 512])),
                waits=[tok])
            ab.rd(v)
            ao_stored.append(v)

    emit_stats_part(0)
    emit_norm_part(0)
    ti_state = {"p1": 0, "p2": 0}

    def bg_early(g):
        def run():
            if g + 1 < NG:
                emit_stats_part(g + 1)
        return run

    def bg_late(g):
        def run():
            if g + 1 < NG:
                emit_norm_part(g + 1)
            if g < 4:
                n1 = (len(p1_tasks) + 3) // 4
                for _ in range(n1):
                    if ti_state["p1"] < len(p1_tasks):
                        emit_p1(p1_tasks[ti_state["p1"]], ti_state["p1"])
                        ti_state["p1"] += 1
            if g == 4:
                emit_ar_chain()
            if 5 <= g <= 10:
                n2 = (len(p2_tasks) + 5) // 6
                for _ in range(n2):
                    if ti_state["p2"] < len(p2_tasks):
                        emit_p2(p2_tasks[ti_state["p2"]], ti_state["p2"])
                        ti_state["p2"] += 1
            if g == 11:
                emit_ag("w1")
            if g == 12:
                emit_ag("w3")
            if g == 13:
                emit_ag("w2")
        return run

    for g in range(NG):
        emit_qkv(g)
        nkc = 4 * g + 4
        k1 = min(2, nkc - 2)
        k2 = nkc - 1
        emit_flash(g, bg={k1: bg_early(g), k2: bg_late(g)})
    c_ao = ccall(lambda e: e.collective_compute(
        "AllGather", ALU.bypass, replica_groups=RGRP,
        ins=[ao_i[:].opt()], outs=[ao_o[:].opt()]),
        waits=list(ao_stored[-8:]) + [("coll", cnt["coll"])])

    actx.close()
    a_end = [(e, cnt[e]) for e in ("pe", "act", "dve", "pool")]

    fctx = ExitStack()
    sbf = lambda name, shape, dt: fctx.enter_context(
        nc.sbuf_tensor(name, shape, dt))
    GH = sbf("GH", [128, NH * 512], BF16)
    AOF = sbf("AOF", [128, NR * 1024], BF16)
    XNTH = sbf("XNTH", [128, NR * 512], BF16)
    W1S = sbf("W1S", [128, 2 * NR * 128], BF16)
    W3S = sbf("W3S", [128, 2 * NR * 128], BF16)
    W2S = sbf("W2S", [128, 2 * NH * 128], BF16)
    WOS = sbf("WOS", [128, 2 * NR * 128], BF16)
    GSIL = sbf("GSIL", [128, 2 * 512], F32)
    PROJT = sbf("PROJT", [128, 2 * 512], F32)
    XFOT = sbf("XFOT", [128, 2 * 512], F32)
    T1T = sbf("T1T", [128, 2 * 512], F32)
    YT = sbf("YT", [128, 2 * 512], F32)
    bGH = Buf()
    bAOF = Buf()
    bXNTH = Buf()
    bW1S = [Buf(), Buf()]
    bW3S = [Buf(), Buf()]
    bW2S = [Buf(), Buf()]
    bWOS = [Buf(), Buf()]
    bGS = [Buf(), Buf()]
    bPJ = [Buf(), Buf()]
    bXO = [Buf(), Buf()]
    bT1 = [Buf(), Buf()]
    bYT = [Buf(), Buf()]
    y_stores = []

    def emit_aof():
        for hl in range(2):
            v = dma("sy", (lambda e, hl=hl: e.dma_start(
                out=AOF[:].rearrange("p (c x) -> p c x", x=2048)
                    [:, :, hl * 1024:(hl + 1) * 1024],
                in_=ao_o.rearrange("(c r) f -> c r f", r=2048)
                    [:, bass.ds(pid_holder["sy"] * 256 + hl * 128, 128), :]
                    .rearrange("c p f -> p c f"))),
                waits=[c_ao])
        bAOF.prod = v

    for th in range(2):
        v = dma("sy", (lambda e, th=th: e.dma_start(
            out=XNTH[:].rearrange("p (r f) -> p r f", f=512),
            in_=xnt.rearrange("(r p) q -> p r q", p=128)
                [:, :, bass.ds(pid_holder["sy"] * 1024 + th * 512, 512)])),
            waits=bXNTH.w() + [xnt_stored[NG - 1]] +
            (a_end if th == 0 else []))
        bXNTH.prod = v
        gh_w = bGH.w()
        for hc in range(NH):
            hw = _hw(hc)
            sl = hc % 2
            v1 = dma("sy", (lambda e, sl=sl, hc=hc, hw=hw: e.dma_start(
                out=W1S[:, sl * 2048:(sl + 1) * 2048]
                    .rearrange("p (r f) -> p r f", f=128)[:, :, 0:hw],
                in_=g_w1o.rearrange("(r p) h -> p r h", p=128)
                    [:, :, hc * 128:hc * 128 + hw])),
                waits=bW1S[sl].w() + [ag_done["w1"]] +
                (a_end if th == 0 and hc == 0 else []))
            bW1S[sl].prod = v1
            v3 = dma("sy", (lambda e, sl=sl, hc=hc, hw=hw: e.dma_start(
                out=W3S[:, sl * 2048:(sl + 1) * 2048]
                    .rearrange("p (r f) -> p r f", f=128)[:, :, 0:hw],
                in_=g_w3o.rearrange("(r p) h -> p r h", p=128)
                    [:, :, hc * 128:hc * 128 + hw])),
                waits=bW3S[sl].w() + [ag_done["w3"]])
            bW3S[sl].prod = v3
            pt = ST[sl]
            pb2 = pbuf[id(pt)]
            mmw = pb2.w()
            for r in range(NR):
                tok = step("pe", (lambda e, pt=pt, r=r, hw=hw, sl=sl:
                           e.matmul(pt[0:hw, 0:512],
                                    W1S[:, sl * 2048 + r * 128:
                                        sl * 2048 + r * 128 + hw],
                                    XNTH[:, r * 512:(r + 1) * 512],
                                    start=(r == 0), stop=(r == NR - 1))),
                    waits=(mmw if r == 0 else []) + [v1] + bXNTH.r())
                bW1S[sl].rd(tok)
                bXNTH.rd(tok)
            for r in range(NR):
                tok = step("pe", (lambda e, pt=pt, r=r, hw=hw, sl=sl:
                           e.matmul(pt[0:hw, 512:1024],
                                    W3S[:, sl * 2048 + r * 128:
                                        sl * 2048 + r * 128 + hw],
                                    XNTH[:, r * 512:(r + 1) * 512],
                                    start=(r == 0), stop=(r == NR - 1))),
                    waits=[v3])
                bW3S[sl].rd(tok)
                bXNTH.rd(tok)
            pb2.prod = tok
            a1 = step("act", (lambda e, pt=pt, hw=hw, sl=sl: e.activation(
                GSIL[0:hw, sl * 512:(sl + 1) * 512], pt[0:hw, 0:512],
                AF.Silu, scale=PARC[0:hw, 4:5])),
                waits=pb2.r() + bGS[sl].w() + bPARC.r())
            bPARC.rd(a1)
            pb2.rd(a1)
            bGS[sl].prod = a1
            d1 = step("dve", (lambda e, pt=pt, hw=hw, hc=hc, sl=sl:
                      e.tensor_tensor(GH[0:hw, hc * 512:(hc + 1) * 512],
                                      GSIL[0:hw, sl * 512:(sl + 1) * 512],
                                      pt[0:hw, 512:1024], ALU.mult)),
                waits=bGS[sl].r() + pb2.r() + (gh_w if hc == 0 else []))
            bGS[sl].rd(d1)
            pb2.rd(d1)
        bGH.prod = d1
        if th == 0:
            emit_aof()
        for dc in range(NR):
            sl = dc % 2
            vwo = dma("sy", (lambda e, sl=sl, dc=dc: e.dma_start(
                out=WOS[:, sl * 2048:(sl + 1) * 2048]
                    .rearrange("p (r f) -> p r f", f=128),
                in_=woutb.rearrange("(r p) q -> p r q", p=128)
                    [:, :, dc * 128:(dc + 1) * 128])),
                waits=bWOS[sl].w() + [v_wob] +
                (a_end if th == 0 and dc == 0 else []))
            bWOS[sl].prod = vwo
            vw2 = dma("sy", (lambda e, sl=sl, dc=dc: e.dma_start(
                out=W2S[:, sl * NH * 128:(sl + 1) * NH * 128]
                    .rearrange("p (r f) -> p r f", f=128),
                in_=g_w2o.rearrange("(r p) q -> p r q", p=128)
                    [:, :, dc * 128:(dc + 1) * 128])),
                waits=bW2S[sl].w() + [ag_done["w2"]])
            bW2S[sl].prod = vw2
            vxo = dma("sy", (lambda e, sl=sl, dc=dc, th=th: e.dma_start(
                out=XFOT[:, sl * 512:(sl + 1) * 512],
                in_=xfo[dc * 128:(dc + 1) * 128,
                        th * 512:(th + 1) * 512])),
                waits=bXO[sl].w())
            bXO[sl].prod = vxo
            po2 = AO1 if dc % 2 == 0 else PAC2
            pb2 = pbuf[id(po2)]
            mmw = pb2.w()
            for r in range(NR):
                tok = step("pe", (lambda e, po2=po2, r=r, th=th, sl=sl:
                           e.matmul(po2[:, 0:512],
                                    WOS[:, sl * 2048 + r * 128:
                                        sl * 2048 + (r + 1) * 128],
                                    AOF[:, r * 1024 + th * 512:
                                        r * 1024 + (th + 1) * 512],
                                    start=(r == 0), stop=(r == NR - 1))),
                    waits=(mmw if r == 0 else []) + [vwo] + bAOF.r())
                bWOS[sl].rd(tok)
                bAOF.rd(tok)
            pb2.prod = tok
            pj = step("act", (lambda e, po2=po2, sl=sl: e.activation(
                PROJT[:, sl * 512:(sl + 1) * 512], po2[:, 0:512], AF.Copy)),
                waits=pb2.r() + bPJ[sl].w())
            pb2.rd(pj)
            bPJ[sl].prod = pj
            po = AO0 if dc % 2 == 0 else PAC
            pb3 = pbuf[id(po)]
            mmw = pb3.w()
            for hc in range(NH):
                hw = _hw(hc)
                tok = step("pe", (lambda e, po=po, hc=hc, hw=hw, sl=sl:
                           e.matmul(po[:, 0:512],
                                    W2S[0:hw, sl * NH * 128 + hc * 128:
                                        sl * NH * 128 + (hc + 1) * 128],
                                    GH[0:hw, hc * 512:(hc + 1) * 512],
                                    start=(hc == 0), stop=(hc == NH - 1))),
                    waits=(mmw if hc == 0 else []) + [vw2] + bGH.r())
                bW2S[sl].rd(tok)
                bGH.rd(tok)
            pb3.prod = tok
            t1 = step("dve", (lambda e, po=po, sl=sl:
                      e.scalar_tensor_tensor(
                          T1T[:, sl * 512:(sl + 1) * 512], po[:, 0:512],
                          S23C[:], PROJT[:, sl * 512:(sl + 1) * 512],
                          ALU.mult, ALU.add)),
                waits=pb3.r() + bPJ[sl].r() + bS23.r() + bT1[sl].w())
            pb3.rd(t1)
            bPJ[sl].rd(t1)
            bS23.rd(t1)
            bT1[sl].prod = t1
            p1 = step("pool", (lambda e, sl=sl: e.tensor_tensor(
                YT[:, sl * 512:(sl + 1) * 512],
                T1T[:, sl * 512:(sl + 1) * 512],
                XFOT[:, sl * 512:(sl + 1) * 512], ALU.add)),
                waits=bT1[sl].r() + bXO[sl].r() + bYT[sl].w())
            bT1[sl].rd(p1)
            bXO[sl].rd(p1)
            bYT[sl].prod = p1
            v = dma("sy", (lambda e, sl=sl, dc=dc, th=th: e.dma_start(
                out=y[dc * 128:(dc + 1) * 128, th * 512:(th + 1) * 512],
                in_=YT[:, sl * 512:(sl + 1) * 512])),
                waits=bYT[sl].r())
            bYT[sl].rd(v)
            y_stores.append(v)
    fctx.close()

    y_final = y_stores[-1]

    with ctx:
        sems = {}
        with ExitStack() as sctx:
            names = ["pe", "act", "dve", "pool", "coll"]
            names += ["sy%d" % i for i in range(8)]
            names += ["gp%d" % i for i in range(4)]
            for s_ in names:
                sems[s_] = sctx.enter_context(nc.semaphore(s_))
            with nc.Block() as block:
                def runner(name):
                    def run(eng):
                        if name in ("sy", "gp"):
                            pid_holder[name] = eng.snap(
                                eng.partition_id(), min_val=0, max_val=7)
                        seen = {}
                        for waits, fn, incs in sched[name]:
                            for s_, v in waits:
                                if seen.get(s_, 0) < v:
                                    eng.wait_ge(sems[s_], v)
                                    seen[s_] = v
                            ins = fn(eng)
                            for s_, v in incs:
                                ins.then_inc(sems[s_], v)
                        if name == "sy":
                            eng.wait_ge(sems[y_final[0]], y_final[1])
                    return run
                block.sync(runner("sy"))
                block.gpsimd(runner("gp"))
                block.tensor(runner("pe"))
                block.scalar(runner("act"))
                block.vector(runner("dve"))
    return nc


_NC_CACHE = [None]


def kernel(L0_attn_x, L0_attn_w_qkv, L0_attn_w_out, L0_ffn_w1, L0_ffn_w3,
           L0_ffn_w2):
    x = np.ascontiguousarray(np.asarray(L0_attn_x, np.float32)[0])
    x_fm = np.ascontiguousarray(x.T)
    wq = np.asarray(L0_attn_w_qkv, np.float32)
    wo = np.ascontiguousarray(np.asarray(L0_attn_w_out, np.float32))
    w1t = np.ascontiguousarray(np.asarray(L0_ffn_w1, np.float32).T)
    w3t = np.ascontiguousarray(np.asarray(L0_ffn_w3, np.float32).T)
    w2t = np.asarray(L0_ffn_w2, np.float32).T
    w2tp = np.zeros((HPAD, D), np.float32)
    w2tp[:HID] = w2t
    idn = np.eye(128, dtype=np.float32)
    masks = np.zeros((4, 128, 512), dtype=ml_dtypes.bfloat16)
    kl = np.arange(128)[:, None]
    ql = np.arange(512)[None, :]
    for j in range(4):
        masks[j] = (ql >= kl + 128 * j).astype(ml_dtypes.bfloat16)

    if _NC_CACHE[0] is None:
        _NC_CACHE[0] = build_program()
    nc = _NC_CACHE[0]

    in_maps = []
    for c in range(NCORE):
        qcols = np.concatenate([wq[:, 256 * c:256 * (c + 1)],
                                wq[:, 2048 + 256 * c:2048 + 256 * (c + 1)],
                                wq[:, 4096 + 256 * c:4096 + 256 * (c + 1)]],
                               axis=1)
        in_maps.append({
            "xtm": x,
            "xfm": x_fm,
            "xfo": np.ascontiguousarray(x_fm[:, 1024 * c:1024 * (c + 1)]),
            "wqkv": np.ascontiguousarray(qcols),
            "wout": wo,
            "w1sl": np.ascontiguousarray(w1t[256 * c:256 * (c + 1), :]),
            "w3sl": np.ascontiguousarray(w3t[256 * c:256 * (c + 1), :]),
            "w2sl": np.ascontiguousarray(w2tp[688 * c:688 * (c + 1), :]),
            "idn": idn,
            "idnn": (idn / 2048.0).astype(np.float32),
            "maskm": masks,
        })
    res = run_bass_kernel_spmd(nc, in_maps, core_ids=list(range(NCORE)))
    out = np.empty((1, S, D), np.float32)
    for c in range(NCORE):
        out[0, 1024 * c:1024 * (c + 1), :] = \
            np.asarray(res.results[c]["y"]).T
    return out


# version 15
# speedup vs baseline: 1.1220x; 1.1220x over previous
import numpy as np
import ml_dtypes
from contextlib import ExitStack

import concourse.bass as bass
import concourse.mybir as mybir
from concourse.bass_utils import run_bass_kernel_spmd

F32 = mybir.dt.float32
BF16 = mybir.dt.bfloat16
AF = mybir.ActivationFunctionType
ALU = mybir.AluOpType
AX = mybir.AxisListType

NCORE = 8
S, D, HID, HPAD = 8192, 2048, 5461, 5504
NW = HID * D
NG, GRP = 16, 512
NH = 43
NR = 16
RSQ = float(np.float32(1.0) / np.sqrt(np.float32(128.0)))
EPS = 1e-5
RGRP = [list(range(NCORE))]


def _hw(hc):
    return 85 if hc == NH - 1 else 128


def build_program():
    nc = bass.Bass("TRN2", target_bir_lowering=False, debug=False,
                   detect_race_conditions=False)

    xtm = nc.dram_tensor("xtm", [S, D], F32, kind="ExternalInput").ap()
    xfm = nc.dram_tensor("xfm", [D, S], F32, kind="ExternalInput").ap()
    xfo = nc.dram_tensor("xfo", [D, 1024], F32, kind="ExternalInput").ap()
    wqkv = nc.dram_tensor("wqkv", [D, 768], F32, kind="ExternalInput").ap()
    wout = nc.dram_tensor("wout", [D, D], F32, kind="ExternalInput").ap()
    w1sl = nc.dram_tensor("w1sl", [256, HID], F32, kind="ExternalInput").ap()
    w3sl = nc.dram_tensor("w3sl", [256, HID], F32, kind="ExternalInput").ap()
    w2sl = nc.dram_tensor("w2sl", [688, D], F32, kind="ExternalInput").ap()
    idn = nc.dram_tensor("idn", [128, 128], F32, kind="ExternalInput").ap()
    idnn = nc.dram_tensor("idnn", [128, 128], F32, kind="ExternalInput").ap()
    maskm = nc.dram_tensor("maskm", [4, 128, 512], BF16,
                           kind="ExternalInput").ap()
    y = nc.dram_tensor("y", [D, 1024], F32, kind="ExternalOutput").ap()

    g_w1i = nc.dram_tensor("g_w1i", [256, HID], BF16).ap()
    g_w1o = nc.dram_tensor("g_w1o", [D, HID], BF16).ap()
    g_w3i = nc.dram_tensor("g_w3i", [256, HID], BF16).ap()
    g_w3o = nc.dram_tensor("g_w3o", [D, HID], BF16).ap()
    g_w2i = nc.dram_tensor("g_w2i", [688, D], BF16).ap()
    g_w2o = nc.dram_tensor("g_w2o", [HPAD, D], BF16).ap()
    g_si = nc.dram_tensor("g_si", [1, 64], F32).ap()
    g_so = nc.dram_tensor("g_so", [1, 64], F32).ap()
    ao_i = nc.dram_tensor("ao_i", [D, 1024], BF16).ap()
    ao_o = nc.dram_tensor("ao_o", [NCORE * D, 1024], BF16).ap()
    xnt = nc.dram_tensor("xnt", [D, S], BF16).ap()
    woutb = nc.dram_tensor("woutb", [D, D], BF16).ap()

    sched = {e: [] for e in ("sy", "gp", "pe", "act", "dve")}
    cnt = {"sy": 0, "gp": 0, "pe": 0, "act": 0, "dve": 0, "pool": 0,
           "coll": 0}

    def emit(eng, fn, waits=(), incs=()):
        stream = "gp" if eng == "pool" else eng
        sched[stream].append((tuple(waits), fn, tuple(incs)))

    NSEM = {"sy": 8, "gp": 4}

    def dma(q, fn, waits=()):
        cnt[q] += 1
        sem = "%s%d" % (q, cnt[q] % NSEM[q])
        cnt[sem] = cnt.get(sem, 0) + 16
        emit(q, fn, waits, ((sem, 16),))
        return (sem, cnt[sem])

    def step(eng, fn, waits=()):
        cnt[eng] += 1
        emit(eng, fn, waits, ((eng, 1),))
        return (eng, cnt[eng])

    def ccall(fn, waits=()):
        cnt["coll"] += 1
        emit("gp", fn, waits, (("coll", 1),))
        return ("coll", cnt["coll"])

    class Buf:
        __slots__ = ("prod", "cons")

        def __init__(s):
            s.prod = None
            s.cons = {}

        def w(s):
            ws = list(s.cons.items())
            if s.prod:
                ws.append(s.prod)
            s.cons = {}
            return ws

        def r(s):
            return [s.prod] if s.prod else []

        def rd(s, tok):
            sem, v = tok
            if s.cons.get(sem, -1) < v:
                s.cons[sem] = v

    ctx = ExitStack()
    sb = lambda name, shape, dt: ctx.enter_context(
        nc.sbuf_tensor(name, shape, dt))
    ps = lambda name, shape: ctx.enter_context(
        nc.psum_tensor(name, shape, F32))

    ST = [ps("ST0", [128, 1024]), ps("ST1", [128, 1024])]
    AO0 = ps("AO0", [128, 512])
    AO1 = ps("AO1", [128, 512])
    PAC = ps("PAC", [128, 512])
    PAC2 = ps("PAC2", [128, 512])
    bST = [Buf(), Buf()]
    bAO0, bAO1, bPAC, bPAC2 = Buf(), Buf(), Buf(), Buf()
    pbuf = {id(ST[0]): bST[0], id(ST[1]): bST[1], id(AO0): bAO0,
            id(AO1): bAO1, id(PAC): bPAC, id(PAC2): bPAC2}

    I32 = sb("I32", [128, 128], F32)
    I32N = sb("I32N", [128, 128], F32)
    LNN = sb("LNN", [128, 1], F32)
    ONES1 = sb("ONES1", [1, 128], F32)
    ONESC = sb("ONESC", [128, 1], F32)
    EPSC = sb("EPSC", [128, 1], F32)
    MASKS = sb("MASKS", [128, 4 * 512], BF16)
    WQKVb = sb("WQKVb", [128, NR * 6 * 128], BF16)
    SROW = sb("SROW", [1, 64], F32)
    PROW = sb("PROW", [1, 16], F32)
    PARC = sb("PARC", [128, 16], F32)
    S23C = sb("S23C", [128, 1], F32)
    ARROW = sb("ARROW", [1, 64], F32)
    bPARC, bS23, bPROW, bSROW, bAR = Buf(), Buf(), Buf(), Buf(), Buf()

    actx = ExitStack()
    sba = lambda name, shape, dt: actx.enter_context(
        nc.sbuf_tensor(name, shape, dt))
    XT = sba("XT", [128, D], F32)
    XF = sba("XF", [128, NR * GRP], F32)
    XNG = sba("XNG", [128, NR * GRP], BF16)
    TMP = sba("TMP", [128, 3 * GRP], F32)
    KF = [sba("KF0", [128, S], BF16), sba("KF1", [128, S], BF16)]
    VT = [sba("VT0", [128, S], BF16), sba("VT1", [128, S], BF16)]
    QF = sba("QF", [128, 2 * 1024], BF16)
    PB = sba("PB", [128, 3 * 1024], BF16)
    LD = sba("LD", [128, 1024], F32)
    LP = sba("LP", [128, 1024], F32)
    LINV = sba("LINV", [1, 1024], F32)
    LBS = sba("LBS", [128, 1024], F32)
    AOB = sba("AOB", [128, 2 * 1024], BF16)
    AOR = sba("AOR", [128, 1024], F32)
    SUMS = sba("SUMS", [128, 4], F32)
    SQC = sba("SQC", [128, 4], F32)
    T1S = sba("T1S", [128, 4], F32)
    VPE = sba("VPE", [128, 4], F32)
    SD4 = sba("SD4", [128, 4], F32)
    RST = sba("RST", [128, 4], F32)
    M4 = sba("M4", [128, 4], F32)
    MB = sba("MB", [128, 512], F32)
    RB = sba("RB", [128, 512], F32)
    WTl = sba("WTl", [128, 2 * 512], F32)
    NEGT = sba("NEGT", [128, 2 * 1024], BF16)
    TERNT = sba("TERNT", [128, 2 * 512], BF16)
    RCOLS = sba("RCOLS", [128, 72], F32)
    PC = sba("PC", [128, 4], F32)

    bXT = [Buf(), Buf()]
    bXF = [Buf(), Buf()]
    bXNG = Buf()
    bQF = [Buf(), Buf()]
    bPB = [Buf(), Buf(), Buf()]
    bL = Buf()
    bLINV = Buf()
    bLBS = Buf()
    bAOB = [Buf(), Buf()]
    bAOR = Buf()
    tmp_lastmul = [None]
    bKF = {}
    bVT = {}
    bMB = Buf()
    bRB = Buf()
    bSUMS, bSQC, bT1S, bVPE, bSD4, bRST, bM4 = (Buf(), Buf(), Buf(), Buf(),
                                                Buf(), Buf(), Buf())
    bWT = [Buf(), Buf()]
    bNEG = [Buf(), Buf()]
    bTERN = [Buf(), Buf()]
    bRC = Buf()
    bPCb = Buf()

    pid_holder = {}

    step("dve", lambda e: e.memset(ONES1[:], 1.0))
    step("dve", lambda e: e.memset(ONESC[:], 1.0))
    step("dve", lambda e: e.memset(EPSC[:], EPS * float(D) * float(D)))
    v_ar0 = step("dve", lambda e: e.memset(ARROW[:], 0.0))
    v_rc0 = step("dve", lambda e: e.memset(RCOLS[:], 0.0))
    step("dve", lambda e: e.memset(LNN[:], float(np.log(2048.0))))
    bRC.prod = v_rc0
    v_i32 = dma("sy", lambda e: e.dma_start(out=I32[:], in_=idn[:]))
    v_i32n = dma("sy", lambda e: e.dma_start(out=I32N[:], in_=idnn[:]))
    v_msk = dma("sy", lambda e: e.dma_start(
        out=MASKS[:].rearrange("p (m f) -> p m f", f=512),
        in_=maskm.rearrange("m p f -> p m f")))
    v_wq = dma("gp", lambda e: e.dma_start(
        out=WQKVb[:].rearrange("p (r j f) -> p r j f", j=6, f=128),
        in_=wqkv.rearrange("(r p) (j f) -> p r j f", p=128, f=128)))
    v_wob = dma("gp", lambda e: e.dma_start(out=woutb[:], in_=wout[:]))

    def tern_tiles(mat):
        if mat in ("w1", "w3"):
            rows, cols = 256, HID
        else:
            rows, cols = 688, D
        out = []
        for pr0 in range(0, rows, 128):
            prc = min(128, rows - pr0)
            for cc0 in range(0, cols, 512):
                cw = min(512, cols - cc0)
                out.append((mat, pr0, prc, cc0, cw))
        return out

    MATSRC = {"w1": w1sl, "w3": w3sl, "w2": w2sl}
    MATDST = {"w1": g_w1i, "w3": g_w3i, "w2": g_w2i}
    MATIDX = {"w1": 0, "w3": 1, "w2": 2}
    p1_tasks = tern_tiles("w1") + tern_tiles("w3") + tern_tiles("w2")
    p2_tasks = list(p1_tasks)
    p1_done = []
    p2_done = {"w1": None, "w3": None, "w2": None}
    pipe = {"p1L": list(enumerate(p1_tasks)), "p1R": [],
            "p2L": list(enumerate(p2_tasks)), "p2S": [], "p2A": [],
            "p2T": [], "p1": True}

    def pump():
        if pipe["p1"]:
            if pipe["p1R"]:
                ti, task = pipe["p1R"].pop(0)
                emit_p1_red(task, ti)
            if pipe["p1L"]:
                ti, task = pipe["p1L"].pop(0)
                emit_p1_load(task, ti)
                pipe["p1R"].append((ti, task))
        else:
            if pipe["p2T"]:
                ti, task = pipe["p2T"].pop(0)
                emit_p2_store(task, ti)
            if pipe["p2A"]:
                ti, task = pipe["p2A"].pop(0)
                emit_p2_add(task, ti)
                pipe["p2T"].append((ti, task))
            if pipe["p2S"]:
                ti, task = pipe["p2S"].pop(0)
                emit_p2_sign(task, ti)
                pipe["p2A"].append((ti, task))
            if pipe["p2L"]:
                ti, task = pipe["p2L"].pop(0)
                emit_p2_load(task, ti)
                pipe["p2S"].append((ti, task))

    def emit_p1_load(task, ti):
        mat, pr0, prc, cc0, cw = task
        sl = ti % 2
        wb = bWT[sl]
        v = dma("gp", (lambda e, s=sl, m=mat, a=pr0, b=prc, c=cc0, d=cw:
                       e.dma_start(out=WTl[0:b, s * 512:s * 512 + d],
                                   in_=MATSRC[m][a:a + b, c:c + d])),
                waits=wb.w())
        wb.prod = v

    def emit_p1_red(task, ti):
        mat, pr0, prc, cc0, cw = task
        sl = ti % 2
        wb = bWT[sl]
        t = step("dve", (lambda e, s=sl, b=prc, d=cw, i=ti:
                         e.tensor_reduce(RCOLS[0:b, i:i + 1],
                                         WTl[0:b, s * 512:s * 512 + d],
                                         AX.X, ALU.add,
                                         apply_absolute_value=True)),
                 waits=wb.r() + bRC.r())
        wb.rd(t)
        bRC.rd(t)
        p1_done.append(t)

    def emit_ar_chain():
        c_w1 = len(tern_tiles("w1"))
        c_w3 = len(tern_tiles("w3"))
        rng = {"w1": (0, c_w1), "w3": (c_w1, c_w1 + c_w3),
               "w2": (c_w1 + c_w3, len(p1_tasks))}
        toks = []
        for m in ("w1", "w3", "w2"):
            a, b = rng[m]
            i = MATIDX[m]
            t = step("dve", (lambda e, a=a, b=b, i=i:
                             e.tensor_reduce(PC[:, i:i + 1],
                                             RCOLS[:, a:b], AX.X, ALU.add)),
                    waits=list(p1_done))
            toks.append(t)
        bPCb.prod = toks[-1]
        mmw = bPAC2.w()
        for m in ("w1", "w3", "w2"):
            i = MATIDX[m]
            t = step("pe", (lambda e, i=i:
                            e.matmul(PAC2[0:1, i:i + 1], PC[:, i:i + 1],
                                     ONESC[:], start=True, stop=True)),
                     waits=(mmw if i == 0 else []) + toks)
        bPAC2.prod = t
        t2 = step("dve", lambda e: e.tensor_copy(ARROW[0:1, 0:3],
                                                 PAC2[0:1, 0:3]),
                  waits=bPAC2.r() + [v_ar0])
        bPAC2.rd(t2)
        bAR.prod = t2
        v = dma("gp", lambda e: e.dma_start(out=g_si[:], in_=ARROW[:]),
                waits=bAR.r())
        c1 = ccall(lambda e: e.collective_compute(
            "AllReduce", ALU.add, replica_groups=RGRP,
            ins=[g_si[:].opt()], outs=[g_so[:].opt()]), waits=[v])
        v2 = dma("sy", lambda e: e.dma_start(out=SROW[:], in_=g_so[:]),
                 waits=[c1])
        bSROW.prod = v2
        t3 = step("dve", lambda e: e.tensor_scalar(
            PROW[0:1, 0:3], SROW[0:1, 0:3], 0.5 / NW, None, ALU.mult),
            waits=bSROW.r())
        t4 = step("dve", lambda e: e.tensor_scalar(
            PROW[0:1, 4:7], SROW[0:1, 0:3], 0.5 / NW, None, ALU.mult),
            waits=bSROW.r())
        t5 = step("act", lambda e: e.mul(PROW[0:1, 8:11], SROW[0:1, 0:3],
                                         -0.5 / NW), waits=bSROW.r())
        bPROW.prod = t5
        mmw = bPAC.w()
        t6 = step("pe", lambda e: e.matmul(PAC[:, 0:16], ONES1[:],
                                           PROW[0:1, 0:16], start=True,
                                           stop=True),
                  waits=mmw + [t3, t4, t5, v_i32])
        bPAC.prod = t6
        t7 = step("dve", lambda e: e.tensor_copy(PARC[:], PAC[:, 0:16]),
                  waits=bPAC.r())
        bPAC.rd(t7)
        bPARC.prod = t7
        t8 = step("pool", lambda e: e.tensor_tensor(
            S23C[:], PARC[:, 5:6], PARC[:, 6:7], ALU.mult),
            waits=bPARC.r())
        bS23.prod = t8

    def emit_p2_load(task, ti):
        mat, pr0, prc, cc0, cw = task
        sl = ti % 2
        wb = bWT[sl]
        v = dma("gp", (lambda e, s=sl, m=mat, a=pr0, b=prc, c=cc0, d=cw:
                       e.dma_start(out=WTl[0:b, s * 512:s * 512 + d],
                                   in_=MATSRC[m][a:a + b, c:c + d])),
                waits=wb.w())
        wb.prod = v

    def emit_p2_sign(task, ti):
        mat, pr0, prc, cc0, cw = task
        sl = ti % 2
        i = MATIDX[mat]
        wb, nb = bWT[sl], bNEG[sl]
        nw_ = nb.w()
        ta = step("act", (lambda e, s=sl, b=prc, d=cw, i=i:
                          e.activation(NEGT[0:b, s * 1024:s * 1024 + d],
                                       WTl[0:b, s * 512:s * 512 + d],
                                       AF.Sign, bias=PARC[0:b, 8 + i:9 + i])),
                  waits=wb.r() + nw_ + bPARC.r())
        tb_ = step("act", (lambda e, s=sl, b=prc, d=cw, i=i:
                           e.activation(NEGT[0:b, s * 1024 + 512:
                                             s * 1024 + 512 + d],
                                        WTl[0:b, s * 512:s * 512 + d],
                                        AF.Sign, bias=PARC[0:b, i:i + 1])),
                   waits=[])
        wb.rd(tb_)
        nb.prod = tb_
        bPARC.rd(tb_)

    def emit_p2_add(task, ti):
        mat, pr0, prc, cc0, cw = task
        sl = ti % 2
        nb, tb = bNEG[sl], bTERN[sl]
        t2 = step("dve", (lambda e, s=sl, b=prc, d=cw:
                          e.tensor_tensor(
                              TERNT[0:b, s * 512:s * 512 + d],
                              NEGT[0:b, s * 1024:s * 1024 + d],
                              NEGT[0:b, s * 1024 + 512:s * 1024 + 512 + d],
                              ALU.add)),
                  waits=nb.r() + tb.w())
        nb.rd(t2)
        tb.prod = t2

    def emit_p2_store(task, ti):
        mat, pr0, prc, cc0, cw = task
        sl = ti % 2
        tb = bTERN[sl]
        v2 = dma("gp", (lambda e, s=sl, m=mat, a=pr0, b=prc, c=cc0, d=cw:
                        e.dma_start(out=MATDST[m][a:a + b, c:c + d],
                                    in_=TERNT[0:b, s * 512:s * 512 + d])),
                 waits=tb.r())
        tb.rd(v2)
        p2_done[mat] = v2

    ag_done = {}

    def emit_ag(mat):
        src = {"w1": (g_w1i, g_w1o), "w3": (g_w3i, g_w3o),
               "w2": (g_w2i, g_w2o)}[mat]
        k = cnt["coll"]
        c = ccall((lambda e, a=src[0], b=src[1]: e.collective_compute(
            "AllGather", ALU.bypass, replica_groups=RGRP,
            ins=[a[:].opt()], outs=[b[:].opt()])),
            waits=[p2_done[mat], ("coll", k)])
        ag_done[mat] = c

    norm_done = {}
    norm_muls = {}
    xnt_stored = {}

    def emit_stats_part(g):
        red_toks, sq_toks = [], []
        for t in range(4):
            xb = bXT[0]
            v = dma("sy", (lambda e, g=g, t=t: e.dma_start(
                out=XT[:],
                in_=xtm[g * GRP + t * 128: g * GRP + (t + 1) * 128, :])),
                waits=xb.w())
            xb.prod = v
            r1 = step("dve", (lambda e, t=t: e.tensor_reduce(
                SUMS[:, t:t + 1], XT[:], AX.X, ALU.add)),
                waits=xb.r() + (bSUMS.w() if t == 0 else []))
            r2 = step("act", (lambda e, t=t: e.activation(
                TMP[:, 0:1024].bitcast(BF16), XT[:], AF.Square,
                accum_out=SQC[:, t:t + 1])),
                waits=xb.r() + (bSQC.w() if t == 0 else []) +
                ([tmp_lastmul[0]] if tmp_lastmul[0] else []))
            xb.rd(r1)
            xb.rd(r2)
            red_toks.append(r1)
            sq_toks.append(r2)
        bSUMS.prod = red_toks[-1]
        bSQC.prod = sq_toks[-1]
        t1 = step("pool", lambda e: e.tensor_tensor(
            T1S[:], SUMS[:], SUMS[:], ALU.mult),
            waits=bSUMS.r() + bT1S.w())
        bT1S.prod = t1
        t2 = step("dve", lambda e: e.scalar_tensor_tensor(
            VPE[:], SQC[:], float(D), T1S[:], ALU.mult, ALU.subtract),
            waits=bSQC.r() + bT1S.r() + bVPE.w())
        bT1S.rd(t2)
        bVPE.prod = t2
        t3 = step("act", lambda e: e.activation(SD4[:], VPE[:], AF.Ln,
                                                bias=EPSC[:]),
                  waits=bVPE.r() + bSD4.w())
        bVPE.rd(t3)
        bSD4.prod = t3
        t3b = step("dve", lambda e: e.tensor_scalar(
            T1S[:], SD4[:], -0.5, None, ALU.mult),
            waits=bSD4.r() + bT1S.w())
        bSD4.rd(t3b)
        bT1S.prod = t3b
        t4 = step("act", lambda e: e.activation(RST[:], T1S[:], AF.Exp,
                                                bias=LNN[:]),
                  waits=bT1S.r() + bRST.w())
        bT1S.rd(t4)
        bRST.prod = t4

    def emit_norm_part(g):
        mmw = bPAC.w()
        for t in range(4):
            tok = step("pe", (lambda e, t=t: e.matmul(
                PAC[:, t * 128:(t + 1) * 128],
                SUMS[:, t:t + 1].broadcast_to([128, 128]), I32N[:],
                start=True, stop=True)),
                waits=(mmw if t == 0 else []) + bSUMS.r() + [v_i32n])
        bSUMS.rd(tok)
        bPAC.prod = tok
        mmw = bPAC2.w()
        for t in range(4):
            tok = step("pe", (lambda e, t=t: e.matmul(
                PAC2[:, t * 128:(t + 1) * 128],
                RST[:, t:t + 1].broadcast_to([128, 128]), I32[:],
                start=True, stop=True)),
                waits=(mmw if t == 0 else []) + bRST.r())
        bRST.rd(tok)
        bPAC2.prod = tok
        e1 = step("act", lambda e: e.activation(MB[:], PAC[:], AF.Copy),
                  waits=bPAC.r() + bMB.w())
        bPAC.rd(e1)
        bMB.prod = e1
        e2 = step("act", lambda e: e.activation(RB[:], PAC2[:], AF.Copy),
                  waits=bPAC2.r() + bRB.w())
        bPAC2.rd(e2)
        bRB.prod = e2
        vh = [None, None]
        for h in range(2):
            xb = bXF[h]
            vh[h] = dma("sy", (lambda e, g=g, h=h: e.dma_start(
                out=XF[:].rearrange("p (r f) -> p r f", f=GRP)
                    [:, h * 8:(h + 1) * 8, :],
                in_=xfm.rearrange("(r p) q -> p r q", p=128)
                    [:, h * 8:(h + 1) * 8, g * GRP:(g + 1) * GRP])),
                waits=xb.w())
            xb.prod = vh[h]
        xng_w = bXNG.w()
        subs, muls = {}, {}
        order = []
        pend = []
        for r in range(NR):
            order.append(("s", r))
            pend.append(r)
            if r >= 2:
                order.append(("m", pend.pop(0)))
        while pend:
            order.append(("m", pend.pop(0)))
        last_mul = None
        for kind, r in order:
            h = 0 if r < 8 else 1
            if kind == "s":
                tok = step("dve", (lambda e, r=r: e.tensor_tensor(
                    TMP[:, (r % 3) * GRP:(r % 3 + 1) * GRP],
                    XF[:, r * GRP:(r + 1) * GRP], MB[:], ALU.subtract)),
                    waits=bXF[h].r() + bMB.r() + (xng_w if r == 0 else []))
                bXF[h].rd(tok)
                bMB.rd(tok)
                subs[r] = tok
            else:
                tok = step("dve", (lambda e, r=r: e.tensor_tensor(
                    XNG[:, r * GRP:(r + 1) * GRP],
                    TMP[:, (r % 3) * GRP:(r % 3 + 1) * GRP], RB[:],
                    ALU.mult)),
                    waits=bRB.r())
                bRB.rd(tok)
                muls[r] = tok
                last_mul = tok
        bXNG.prod = last_mul
        norm_done[g] = last_mul
        norm_muls[g] = dict(muls)
        tmp_lastmul[0] = last_mul
        v = dma("sy", (lambda e, g=g: e.dma_start(
            out=xnt.rearrange("(r p) q -> p r q", p=128)
                [:, :, g * GRP:(g + 1) * GRP],
            in_=XNG[:].rearrange("p (r f) -> p r f", f=GRP))),
            waits=[last_mul])
        bXNG.rd(v)
        xnt_stored[g] = v

    qkv_last_mm = {}

    def emit_qkv(g):
        nm = norm_muls[g]
        for j in range(4):
            pt = PAC if j % 2 == 0 else PAC2
            pb = pbuf[id(pt)]
            mmw = pb.w()
            for r in range(NR):
                tok = step("pe", (lambda e, pt=pt, r=r, j=j: e.matmul(
                    pt[:], WQKVb[:, (r * 6 + j) * 128:(r * 6 + j + 1) * 128],
                    XNG[:, r * GRP:(r + 1) * GRP],
                    start=(r == 0), stop=(r == NR - 1))),
                    waits=(mmw if r == 0 else []) + [nm[r], v_wq])
                bXNG.rd(tok)
            pb.prod = tok
            if j < 2:
                ev = step("act", (lambda e, pt=pt, g=g, j=j: e.activation(
                    QF[:, (g % 2) * 1024 + j * 512:
                       (g % 2) * 1024 + (j + 1) * 512],
                    pt[:], AF.Copy, scale=RSQ)),
                    waits=pb.r() + (bQF[g % 2].w() if j == 0 else []))
                bQF[g % 2].prod = ev
            else:
                ev = step("act", (lambda e, pt=pt, g=g, j=j: e.activation(
                    KF[j - 2][:, g * GRP:(g + 1) * GRP], pt[:], AF.Copy)),
                    waits=pb.r())
            pb.rd(ev)
        bKF[g] = ev
        for t in range(4):
            pt = PAC if t % 2 == 0 else PAC2
            pb = pbuf[id(pt)]
            mmw = pb.w()
            for r in range(NR):
                tok = step("pe", (lambda e, pt=pt, r=r, t=t: e.matmul(
                    pt[:, 0:256],
                    XNG[:, r * GRP + t * 128:r * GRP + (t + 1) * 128],
                    WQKVb[:, (r * 6 + 4) * 128:(r * 6 + 6) * 128],
                    start=(r == 0), stop=(r == NR - 1))),
                    waits=(mmw if r == 0 else []) + [nm[r], v_wq])
                bXNG.rd(tok)
            pb.prod = tok
            kc = 4 * g + t
            ev1 = step("act", (lambda e, pt=pt, kc=kc: e.activation(
                VT[0][:, kc * 128:(kc + 1) * 128], pt[:, 0:128], AF.Copy)),
                waits=pb.r())
            ev2 = step("act", (lambda e, pt=pt, kc=kc: e.activation(
                VT[1][:, kc * 128:(kc + 1) * 128], pt[:, 128:256], AF.Copy)),
                waits=pb.r())
            pb.rd(ev1)
            pb.rd(ev2)
        bVT[g] = ev2
        qkv_last_mm[g] = tok

    ao_stored = []

    def emit_flash(g, bg=None):
        nkc = 4 * g + 4
        first_l = {"dve": True, "pool": True}
        lw = bL.w()

        def emit_st(kc):
            b = kc % 2
            stw = bST[b].w()
            kv_w = [bKF[kc // 4], bVT[kc // 4]]
            m1 = step("pe", (lambda e, b=b, kc=kc, g=g: e.matmul(
                ST[b][:, 0:512], KF[0][:, kc * 128:(kc + 1) * 128],
                QF[:, (g % 2) * 1024:(g % 2) * 1024 + 512],
                start=True, stop=True)),
                waits=stw + kv_w + bQF[g % 2].r())
            m2 = step("pe", (lambda e, b=b, kc=kc, g=g: e.matmul(
                ST[b][:, 512:1024], KF[1][:, kc * 128:(kc + 1) * 128],
                QF[:, (g % 2) * 1024 + 512:(g % 2) * 1024 + 1024],
                start=True, stop=True)),
                waits=[])
            bQF[g % 2].rd(m2)
            bST[b].prod = m2

        def emit_exp(kc):
            b = kc % 2
            pb = kc % 3
            diag = kc >= 4 * g
            a = step("act", (lambda e, b=b, pb=pb: e.activation(
                PB[:, pb * 1024:(pb + 1) * 1024], ST[b][:], AF.Exp)),
                waits=bST[b].r() + bPB[pb].w())
            bST[b].rd(a)
            bPB[pb].prod = a
            gate = [a]
            if diag:
                j = kc - 4 * g
                d1 = step("dve", (lambda e, pb=pb, j=j: e.tensor_tensor(
                    PB[:, pb * 1024:pb * 1024 + 512],
                    PB[:, pb * 1024:pb * 1024 + 512],
                    MASKS[:, j * 512:(j + 1) * 512], ALU.mult)),
                    waits=[a, v_msk])
                d2 = step("dve", (lambda e, pb=pb, j=j: e.tensor_tensor(
                    PB[:, pb * 1024 + 512:pb * 1024 + 1024],
                    PB[:, pb * 1024 + 512:pb * 1024 + 1024],
                    MASKS[:, j * 512:(j + 1) * 512], ALU.mult)),
                    waits=[])
                bPB[pb].prod = d2
                gate = [d2]
            return gate

        def emit_av_l(kc, gate):
            pb = kc % 3
            av_w = (bAO0.w() + bAO1.w()) if kc == 0 else []
            v1 = step("pe", (lambda e, pb=pb, kc=kc, nkc=nkc: e.matmul(
                AO0[:], VT[0][:, kc * 128:(kc + 1) * 128],
                PB[:, pb * 1024:pb * 1024 + 512],
                start=(kc == 0), stop=(kc == nkc - 1))),
                waits=gate + av_w)
            v2 = step("pe", (lambda e, pb=pb, kc=kc, nkc=nkc: e.matmul(
                AO1[:], VT[1][:, kc * 128:(kc + 1) * 128],
                PB[:, pb * 1024 + 512:pb * 1024 + 1024],
                start=(kc == 0), stop=(kc == nkc - 1))),
                waits=[])
            bPB[pb].rd(v2)
            leng = "pool" if kc % 3 == 2 else "dve"
            LT = LD if leng == "dve" else LP
            if first_l[leng]:
                lt = step(leng, (lambda e, LT=LT, pb=pb: e.tensor_copy(
                    LT[:], PB[:, pb * 1024:(pb + 1) * 1024])),
                    waits=gate + lw)
                first_l[leng] = False
            else:
                lt = step(leng, (lambda e, LT=LT, pb=pb: e.tensor_tensor(
                    LT[:], LT[:], PB[:, pb * 1024:(pb + 1) * 1024],
                    ALU.add)),
                    waits=gate)
            bPB[pb].rd(lt)
            return v1, v2, lt

        l_dve = l_pool = None
        emit_st(0)
        for kc in range(nkc):
            gate = emit_exp(kc)
            if kc + 1 < nkc:
                emit_st(kc + 1)
            v1, v2, lt = emit_av_l(kc, gate)
            if lt[0] == "dve":
                l_dve = lt
            else:
                l_pool = lt
            pump()
            if bg is not None and kc in bg:
                bg[kc]()
        bAO0.prod = v1
        bAO1.prod = v2
        arw = bAOR.w()
        ar1 = step("act", lambda e: e.activation(AOR[:, 0:512], AO0[:],
                                                 AF.Copy),
                   waits=bAO0.r() + arw)
        bAO0.rd(ar1)
        ar2 = step("act", lambda e: e.activation(AOR[:, 512:1024], AO1[:],
                                                 AF.Copy),
                   waits=bAO1.r())
        bAO1.rd(ar2)
        bAOR.prod = ar2
        for h, pt in ((0, PAC2), (1, PAC)):
            pb2 = pbuf[id(pt)]
            mmw = pb2.w()
            t1 = step("pe", (lambda e, pt=pt, h=h: e.matmul(
                pt[0:1, 0:512], ONESC[:], LD[:, h * 512:(h + 1) * 512],
                start=True, stop=False)),
                waits=mmw + [l_dve, l_pool])
            t2 = step("pe", (lambda e, pt=pt, h=h: e.matmul(
                pt[0:1, 0:512], ONESC[:], LP[:, h * 512:(h + 1) * 512],
                start=False, stop=True)),
                waits=[])
            pb2.prod = t2
        bL.rd(t2)
        r1 = step("dve", lambda e: e.reciprocal(LINV[0:1, 0:512],
                                                PAC2[0:1, 0:512]),
                  waits=bPAC2.r() + bLINV.w())
        bPAC2.rd(r1)
        r2 = step("dve", lambda e: e.reciprocal(LINV[0:1, 512:1024],
                                                PAC[0:1, 0:512]),
                  waits=bPAC.r())
        bPAC.rd(r2)
        bLINV.prod = r2
        for h, pt in ((0, PAC2), (1, PAC)):
            pb2 = pbuf[id(pt)]
            mmw = pb2.w()
            t = step("pe", (lambda e, pt=pt, h=h: e.matmul(
                pt[:], ONES1[:], LINV[0:1, h * 512:(h + 1) * 512],
                start=True, stop=True)),
                waits=mmw + bLINV.r())
            bLINV.rd(t)
            pb2.prod = t
        lb_w = bLBS.w()
        e1 = step("dve", lambda e: e.tensor_copy(LBS[:, 0:512],
                                                 PAC2[:, 0:512]),
                  waits=bPAC2.r() + lb_w)
        bPAC2.rd(e1)
        e2 = step("dve", lambda e: e.tensor_copy(LBS[:, 512:1024],
                                                 PAC[:, 0:512]),
                  waits=bPAC.r())
        bPAC.rd(e2)
        bLBS.prod = e2
        ab = bAOB[g % 2]
        abw = ab.w()
        n1 = step("dve", (lambda e, g=g: e.tensor_tensor(
            AOB[:, (g % 2) * 1024:(g % 2) * 1024 + 512], AOR[:, 0:512],
            LBS[:, 0:512], ALU.mult)),
            waits=bAOR.r() + bLBS.r() + abw)
        bAOR.rd(n1)
        n2 = step("dve", (lambda e, g=g: e.tensor_tensor(
            AOB[:, (g % 2) * 1024 + 512:(g % 2) * 1024 + 1024],
            AOR[:, 512:1024],
            LBS[:, 512:1024], ALU.mult)),
            waits=bAOR.r() + bLBS.r())
        bAOR.rd(n2)
        bLBS.rd(n2)
        ab.prod = n2
        j = g // 2
        for hl, tok in ((0, n1), (1, n2)):
            v = dma("sy", (lambda e, g=g, j=j, hl=hl: e.dma_start(
                out=ao_i[j * 256 + hl * 128:j * 256 + (hl + 1) * 128,
                         (g % 2) * 512:(g % 2 + 1) * 512],
                in_=AOB[:, (g % 2) * 1024 + hl * 512:
                        (g % 2) * 1024 + (hl + 1) * 512])),
                waits=[tok])
            ab.rd(v)
            ao_stored.append(v)

    emit_stats_part(0)
    emit_norm_part(0)

    def bg_early(g):
        def run():
            if g + 1 < NG:
                emit_stats_part(g + 1)
        return run

    def bg_late(g):
        def run():
            if g + 1 < NG:
                emit_norm_part(g + 1)
            if g == 5:
                while pipe["p1L"] or pipe["p1R"]:
                    pump()
                emit_ar_chain()
                pipe["p1"] = False
            if g == 12:
                while any(pipe[k] for k in ("p2L", "p2S", "p2A", "p2T")):
                    pump()
            if g == 13:
                emit_ag("w1")
            if g == 14:
                emit_ag("w3")
            if g == 15:
                emit_ag("w2")
        return run

    for g in range(NG):
        emit_qkv(g)
        nkc = 4 * g + 4
        k1 = min(2, nkc - 2)
        k2 = nkc - 1
        emit_flash(g, bg={k1: bg_early(g), k2: bg_late(g)})
    c_ao = ccall(lambda e: e.collective_compute(
        "AllGather", ALU.bypass, replica_groups=RGRP,
        ins=[ao_i[:].opt()], outs=[ao_o[:].opt()]),
        waits=list(ao_stored[-8:]) + [("coll", cnt["coll"])])

    actx.close()
    a_end = [(e, cnt[e]) for e in ("pe", "act", "dve", "pool")]

    fctx = ExitStack()
    sbf = lambda name, shape, dt: fctx.enter_context(
        nc.sbuf_tensor(name, shape, dt))
    GH = sbf("GH", [128, NH * 512], BF16)
    AOF = sbf("AOF", [128, NR * 1024], BF16)
    XNTH = sbf("XNTH", [128, NR * 512], BF16)
    W1S = sbf("W1S", [128, 2 * NR * 128], BF16)
    W3S = sbf("W3S", [128, 2 * NR * 128], BF16)
    W2S = sbf("W2S", [128, 2 * NH * 128], BF16)
    WOS = sbf("WOS", [128, 2 * NR * 128], BF16)
    GSIL = sbf("GSIL", [128, 2 * 512], F32)
    PROJT = sbf("PROJT", [128, 2 * 512], F32)
    XFOT = sbf("XFOT", [128, 2 * 512], F32)
    T1T = sbf("T1T", [128, 2 * 512], F32)
    YT = sbf("YT", [128, 2 * 512], F32)
    bGH = Buf()
    bAOF = Buf()
    bXNTH = Buf()
    bW1S = [Buf(), Buf()]
    bW3S = [Buf(), Buf()]
    bW2S = [Buf(), Buf()]
    bWOS = [Buf(), Buf()]
    bGS = [Buf(), Buf()]
    bPJ = [Buf(), Buf()]
    bXO = [Buf(), Buf()]
    bT1 = [Buf(), Buf()]
    bYT = [Buf(), Buf()]
    y_stores = []

    def emit_aof():
        for hl in range(2):
            v = dma("sy", (lambda e, hl=hl: e.dma_start(
                out=AOF[:].rearrange("p (c x) -> p c x", x=2048)
                    [:, :, hl * 1024:(hl + 1) * 1024],
                in_=ao_o.rearrange("(c r) f -> c r f", r=2048)
                    [:, bass.ds(pid_holder["sy"] * 256 + hl * 128, 128), :]
                    .rearrange("c p f -> p c f"))),
                waits=[c_ao])
        bAOF.prod = v

    for th in range(2):
        v = dma("sy", (lambda e, th=th: e.dma_start(
            out=XNTH[:].rearrange("p (r f) -> p r f", f=512),
            in_=xnt.rearrange("(r p) q -> p r q", p=128)
                [:, :, bass.ds(pid_holder["sy"] * 1024 + th * 512, 512)])),
            waits=bXNTH.w() + [xnt_stored[NG - 1]] +
            (a_end if th == 0 else []))
        bXNTH.prod = v
        gh_w = bGH.w()
        for hc in range(NH):
            hw = _hw(hc)
            sl = hc % 2
            v1 = dma("sy", (lambda e, sl=sl, hc=hc, hw=hw: e.dma_start(
                out=W1S[:, sl * 2048:(sl + 1) * 2048]
                    .rearrange("p (r f) -> p r f", f=128)[:, :, 0:hw],
                in_=g_w1o.rearrange("(r p) h -> p r h", p=128)
                    [:, :, hc * 128:hc * 128 + hw])),
                waits=bW1S[sl].w() + [ag_done["w1"]] +
                (a_end if th == 0 and hc == 0 else []))
            bW1S[sl].prod = v1
            v3 = dma("sy", (lambda e, sl=sl, hc=hc, hw=hw: e.dma_start(
                out=W3S[:, sl * 2048:(sl + 1) * 2048]
                    .rearrange("p (r f) -> p r f", f=128)[:, :, 0:hw],
                in_=g_w3o.rearrange("(r p) h -> p r h", p=128)
                    [:, :, hc * 128:hc * 128 + hw])),
                waits=bW3S[sl].w() + [ag_done["w3"]])
            bW3S[sl].prod = v3
            pt = ST[sl]
            pb2 = pbuf[id(pt)]
            mmw = pb2.w()
            for r in range(NR):
                tok = step("pe", (lambda e, pt=pt, r=r, hw=hw, sl=sl:
                           e.matmul(pt[0:hw, 0:512],
                                    W1S[:, sl * 2048 + r * 128:
                                        sl * 2048 + r * 128 + hw],
                                    XNTH[:, r * 512:(r + 1) * 512],
                                    start=(r == 0), stop=(r == NR - 1))),
                    waits=(mmw if r == 0 else []) + [v1] + bXNTH.r())
                bW1S[sl].rd(tok)
                bXNTH.rd(tok)
            for r in range(NR):
                tok = step("pe", (lambda e, pt=pt, r=r, hw=hw, sl=sl:
                           e.matmul(pt[0:hw, 512:1024],
                                    W3S[:, sl * 2048 + r * 128:
                                        sl * 2048 + r * 128 + hw],
                                    XNTH[:, r * 512:(r + 1) * 512],
                                    start=(r == 0), stop=(r == NR - 1))),
                    waits=[v3])
                bW3S[sl].rd(tok)
                bXNTH.rd(tok)
            pb2.prod = tok
            a1 = step("act", (lambda e, pt=pt, hw=hw, sl=sl: e.activation(
                GSIL[0:hw, sl * 512:(sl + 1) * 512], pt[0:hw, 0:512],
                AF.Silu, scale=PARC[0:hw, 4:5])),
                waits=pb2.r() + bGS[sl].w() + bPARC.r())
            bPARC.rd(a1)
            pb2.rd(a1)
            bGS[sl].prod = a1
            d1 = step("dve", (lambda e, pt=pt, hw=hw, hc=hc, sl=sl:
                      e.tensor_tensor(GH[0:hw, hc * 512:(hc + 1) * 512],
                                      GSIL[0:hw, sl * 512:(sl + 1) * 512],
                                      pt[0:hw, 512:1024], ALU.mult)),
                waits=bGS[sl].r() + pb2.r() + (gh_w if hc == 0 else []))
            bGS[sl].rd(d1)
            pb2.rd(d1)
        bGH.prod = d1
        if th == 0:
            emit_aof()
        for dc in range(NR):
            sl = dc % 2
            vwo = dma("sy", (lambda e, sl=sl, dc=dc: e.dma_start(
                out=WOS[:, sl * 2048:(sl + 1) * 2048]
                    .rearrange("p (r f) -> p r f", f=128),
                in_=woutb.rearrange("(r p) q -> p r q", p=128)
                    [:, :, dc * 128:(dc + 1) * 128])),
                waits=bWOS[sl].w() + [v_wob] +
                (a_end if th == 0 and dc == 0 else []))
            bWOS[sl].prod = vwo
            vw2 = dma("sy", (lambda e, sl=sl, dc=dc: e.dma_start(
                out=W2S[:, sl * NH * 128:(sl + 1) * NH * 128]
                    .rearrange("p (r f) -> p r f", f=128),
                in_=g_w2o.rearrange("(r p) q -> p r q", p=128)
                    [:, :, dc * 128:(dc + 1) * 128])),
                waits=bW2S[sl].w() + [ag_done["w2"]])
            bW2S[sl].prod = vw2
            vxo = dma("sy", (lambda e, sl=sl, dc=dc, th=th: e.dma_start(
                out=XFOT[:, sl * 512:(sl + 1) * 512],
                in_=xfo[dc * 128:(dc + 1) * 128,
                        th * 512:(th + 1) * 512])),
                waits=bXO[sl].w())
            bXO[sl].prod = vxo
            po2 = AO1 if dc % 2 == 0 else PAC2
            pb2 = pbuf[id(po2)]
            mmw = pb2.w()
            for r in range(NR):
                tok = step("pe", (lambda e, po2=po2, r=r, th=th, sl=sl:
                           e.matmul(po2[:, 0:512],
                                    WOS[:, sl * 2048 + r * 128:
                                        sl * 2048 + (r + 1) * 128],
                                    AOF[:, r * 1024 + th * 512:
                                        r * 1024 + (th + 1) * 512],
                                    start=(r == 0), stop=(r == NR - 1))),
                    waits=(mmw if r == 0 else []) + [vwo] + bAOF.r())
                bWOS[sl].rd(tok)
                bAOF.rd(tok)
            pb2.prod = tok
            pj = step("act", (lambda e, po2=po2, sl=sl: e.activation(
                PROJT[:, sl * 512:(sl + 1) * 512], po2[:, 0:512], AF.Copy)),
                waits=pb2.r() + bPJ[sl].w())
            pb2.rd(pj)
            bPJ[sl].prod = pj
            po = AO0 if dc % 2 == 0 else PAC
            pb3 = pbuf[id(po)]
            mmw = pb3.w()
            for hc in range(NH):
                hw = _hw(hc)
                tok = step("pe", (lambda e, po=po, hc=hc, hw=hw, sl=sl:
                           e.matmul(po[:, 0:512],
                                    W2S[0:hw, sl * NH * 128 + hc * 128:
                                        sl * NH * 128 + (hc + 1) * 128],
                                    GH[0:hw, hc * 512:(hc + 1) * 512],
                                    start=(hc == 0), stop=(hc == NH - 1))),
                    waits=(mmw if hc == 0 else []) + [vw2] + bGH.r())
                bW2S[sl].rd(tok)
                bGH.rd(tok)
            pb3.prod = tok
            t1 = step("dve", (lambda e, po=po, sl=sl:
                      e.scalar_tensor_tensor(
                          T1T[:, sl * 512:(sl + 1) * 512], po[:, 0:512],
                          S23C[:], PROJT[:, sl * 512:(sl + 1) * 512],
                          ALU.mult, ALU.add)),
                waits=pb3.r() + bPJ[sl].r() + bS23.r() + bT1[sl].w())
            pb3.rd(t1)
            bPJ[sl].rd(t1)
            bS23.rd(t1)
            bT1[sl].prod = t1
            p1 = step("pool", (lambda e, sl=sl: e.tensor_tensor(
                YT[:, sl * 512:(sl + 1) * 512],
                T1T[:, sl * 512:(sl + 1) * 512],
                XFOT[:, sl * 512:(sl + 1) * 512], ALU.add)),
                waits=bT1[sl].r() + bXO[sl].r() + bYT[sl].w())
            bT1[sl].rd(p1)
            bXO[sl].rd(p1)
            bYT[sl].prod = p1
            v = dma("sy", (lambda e, sl=sl, dc=dc, th=th: e.dma_start(
                out=y[dc * 128:(dc + 1) * 128, th * 512:(th + 1) * 512],
                in_=YT[:, sl * 512:(sl + 1) * 512])),
                waits=bYT[sl].r())
            bYT[sl].rd(v)
            y_stores.append(v)
    fctx.close()

    y_final = y_stores[-1]

    with ctx:
        sems = {}
        with ExitStack() as sctx:
            names = ["pe", "act", "dve", "pool", "coll"]
            names += ["sy%d" % i for i in range(8)]
            names += ["gp%d" % i for i in range(4)]
            for s_ in names:
                sems[s_] = sctx.enter_context(nc.semaphore(s_))
            with nc.Block() as block:
                def runner(name):
                    def run(eng):
                        if name in ("sy", "gp"):
                            pid_holder[name] = eng.snap(
                                eng.partition_id(), min_val=0, max_val=7)
                        seen = {}
                        for waits, fn, incs in sched[name]:
                            for s_, v in waits:
                                if seen.get(s_, 0) < v:
                                    eng.wait_ge(sems[s_], v)
                                    seen[s_] = v
                            ins = fn(eng)
                            for s_, v in incs:
                                ins.then_inc(sems[s_], v)
                        if name == "sy":
                            eng.wait_ge(sems[y_final[0]], y_final[1])
                    return run
                block.sync(runner("sy"))
                block.gpsimd(runner("gp"))
                block.tensor(runner("pe"))
                block.scalar(runner("act"))
                block.vector(runner("dve"))
    return nc


_NC_CACHE = [None]


def kernel(L0_attn_x, L0_attn_w_qkv, L0_attn_w_out, L0_ffn_w1, L0_ffn_w3,
           L0_ffn_w2):
    x = np.ascontiguousarray(np.asarray(L0_attn_x, np.float32)[0])
    x_fm = np.ascontiguousarray(x.T)
    wq = np.asarray(L0_attn_w_qkv, np.float32)
    wo = np.ascontiguousarray(np.asarray(L0_attn_w_out, np.float32))
    w1t = np.ascontiguousarray(np.asarray(L0_ffn_w1, np.float32).T)
    w3t = np.ascontiguousarray(np.asarray(L0_ffn_w3, np.float32).T)
    w2t = np.asarray(L0_ffn_w2, np.float32).T
    w2tp = np.zeros((HPAD, D), np.float32)
    w2tp[:HID] = w2t
    idn = np.eye(128, dtype=np.float32)
    masks = np.zeros((4, 128, 512), dtype=ml_dtypes.bfloat16)
    kl = np.arange(128)[:, None]
    ql = np.arange(512)[None, :]
    for j in range(4):
        masks[j] = (ql >= kl + 128 * j).astype(ml_dtypes.bfloat16)

    if _NC_CACHE[0] is None:
        _NC_CACHE[0] = build_program()
    nc = _NC_CACHE[0]

    in_maps = []
    for c in range(NCORE):
        qcols = np.concatenate([wq[:, 256 * c:256 * (c + 1)],
                                wq[:, 2048 + 256 * c:2048 + 256 * (c + 1)],
                                wq[:, 4096 + 256 * c:4096 + 256 * (c + 1)]],
                               axis=1)
        in_maps.append({
            "xtm": x,
            "xfm": x_fm,
            "xfo": np.ascontiguousarray(x_fm[:, 1024 * c:1024 * (c + 1)]),
            "wqkv": np.ascontiguousarray(qcols),
            "wout": wo,
            "w1sl": np.ascontiguousarray(w1t[256 * c:256 * (c + 1), :]),
            "w3sl": np.ascontiguousarray(w3t[256 * c:256 * (c + 1), :]),
            "w2sl": np.ascontiguousarray(w2tp[688 * c:688 * (c + 1), :]),
            "idn": idn,
            "idnn": (idn / 2048.0).astype(np.float32),
            "maskm": masks,
        })
    res = run_bass_kernel_spmd(nc, in_maps, core_ids=list(range(NCORE)))
    out = np.empty((1, S, D), np.float32)
    for c in range(NCORE):
        out[0, 1024 * c:1024 * (c + 1), :] = \
            np.asarray(res.results[c]["y"]).T
    return out
